# version 01
# baseline (speedup 1.0000x reference)
import numpy as np
import ml_dtypes
import concourse.bass as bass
import concourse.mybir as mybir
from concourse.bass import IndirectOffsetOnAxis
from concourse.bass_utils import run_bass_kernel_spmd
from contextlib import ExitStack

F32 = mybir.dt.float32
BF16 = mybir.dt.bfloat16
U32 = mybir.dt.uint32
U8 = mybir.dt.uint8
AF = mybir.ActivationFunctionType
ALU = mybir.AluOpType
AX = mybir.AxisListType

NCORES = 8
TPC = 2048
SEQ = 8192
D = 1024
DC = 8
E = 8
FF = 4096
FC = 32
CAP = 2048
PAD = 384
KG = 8
KT = 8
QT = 16
SCALE = 1.0 / 32.0
EPS = 1e-5
BIG = float(0x3FFFFF)
BFD = ml_dtypes.bfloat16


class P:
    ENGS = ("pe", "act", "dve", "sync", "gp")

    def __init__(self, nc, es):
        self.nc = nc
        self.streams = {k: [] for k in self.ENGS}
        self.sem = {n: es.enter_context(nc.semaphore("s_" + n))
                    for n in ("lds", "ldg", "pe", "act", "dve", "gp")}
        self.cnt = {n: 0 for n in self.sem}
        self.waited = {e: {n: 0 for n in self.sem} for e in self.ENGS}
        self.res = {}
        self.eng2sem = {"pe": "pe", "act": "act", "dve": "dve", "gp": "gp"}

    def emit(self, eng, fn):
        self.streams[eng].append(fn)

    def wait(self, eng, semn, n):
        if n is None or n <= 0 or self.waited[eng][semn] >= n:
            return
        self.waited[eng][semn] = n
        s = self.sem[semn]
        self.emit(eng, lambda e, s=s, n=n: e.wait_ge(s, n))

    def instr(self, eng, fn, mark=True):
        if not mark:
            self.emit(eng, fn)
            return self.cnt.get(self.eng2sem.get(eng), 0)
        semn = self.eng2sem[eng]
        self.cnt[semn] += 1
        s = self.sem[semn]
        self.emit(eng, lambda e, fn=fn, s=s: fn(e).then_inc(s, 1))
        return self.cnt[semn]

    def chain(self, eng):
        semn = self.eng2sem[eng]
        self.wait(eng, semn, self.cnt[semn])

    def dma(self, eng, out, in_, **kw):
        semn = {"sync": "lds", "gp": "ldg"}[eng]
        self.cnt[semn] += 16
        s = self.sem[semn]
        self.emit(eng, lambda e, out=out, in_=in_, kw=kw, s=s:
                  e.dma_start(out=out, in_=in_, **kw).then_inc(s, 16))
        return self.cnt[semn]

    def use(self, name, semn, n):
        self.res[name] = (semn, n)

    def guard(self, eng, name):
        if name in self.res:
            semn, n = self.res[name]
            self.wait(eng, semn, n)

    def build(self, block):
        for eng_name, attr in (("pe", "tensor"), ("act", "scalar"),
                               ("dve", "vector"), ("sync", "sync"), ("gp", "gpsimd")):
            stream = self.streams[eng_name]
            dec = getattr(block, attr)

            def body(engine, stream=stream):
                for fn in stream:
                    fn(engine)
            dec(body)


def emit_ln_tile(p, buf_ap, sc):
    p.chain("dve")
    p.instr("dve", lambda e: e.tensor_reduce(sc["s"][:], buf_ap, AX.X, ALU.add))
    p.instr("dve", lambda e: e.scalar_tensor_tensor(
        sc["sq"][:], buf_ap, 1.0, buf_ap, ALU.mult, ALU.mult, accum_out=sc["s2"][:]))
    p.chain("dve")
    p.instr("dve", lambda e: e.tensor_scalar(sc["m"][:], sc["s"][:], 1.0 / D, None, ALU.mult))
    p.chain("dve")
    p.instr("dve", lambda e: e.tensor_tensor(sc["mm"][:], sc["m"][:], sc["m"][:], ALU.mult))
    p.chain("dve")
    nv = p.instr("dve", lambda e: e.scalar_tensor_tensor(
        sc["var"][:], sc["s2"][:], 1.0 / D, sc["mm"][:], ALU.mult, ALU.subtract))
    p.wait("act", "dve", nv)
    na = p.instr("act", lambda e: e.activation(sc["std"][:], sc["var"][:], AF.Sqrt,
                                               bias=sc["eps"][:], scale=1.0))
    p.wait("dve", "act", na)
    p.instr("dve", lambda e: e.reciprocal(sc["rstd"][:], sc["std"][:]))
    p.chain("dve")
    nf = p.instr("dve", lambda e: e.tensor_scalar(
        buf_ap, buf_ap, sc["m"][:], sc["rstd"][:], ALU.subtract, ALU.mult))
    return nf


DEBUG_LA = False


def build_la():
    nc = bass.Bass("TRN2", target_bir_lowering=False, debug=False)
    x_batch = nc.dram_tensor("x_batch", [SEQ, D], F32, kind="ExternalInput").ap()
    x_own = nc.dram_tensor("x_own", [TPC, D], F32, kind="ExternalInput").ap()
    Wq = nc.dram_tensor("Wq", [D, D], F32, kind="ExternalInput").ap()
    bq_pc = nc.dram_tensor("bq_pc", [128, DC], F32, kind="ExternalInput").ap()
    Wr = nc.dram_tensor("Wr", [D, E], F32, kind="ExternalInput").ap()
    c_identf = nc.dram_tensor("c_identf", [128, 128], F32, kind="ExternalInput").ap()
    c_identb = nc.dram_tensor("c_identb", [128, 128], BF16, kind="ExternalInput").ap()
    c_ltrib = nc.dram_tensor("c_ltrib", [128, 128], BF16, kind="ExternalInput").ap()
    c_onescb = nc.dram_tensor("c_onescb", [128, 1], BF16, kind="ExternalInput").ap()
    c_onesrf = nc.dram_tensor("c_onesrf", [1, 128], F32, kind="ExternalInput").ap()
    c_iota8 = nc.dram_tensor("c_iota8", [128, E], F32, kind="ExternalInput").ap()
    c_eights = nc.dram_tensor("c_eights", [128, E], F32, kind="ExternalInput").ap()
    c_big = nc.dram_tensor("c_big", [128, 1], F32, kind="ExternalInput").ap()

    o_x1 = nc.dram_tensor("x1", [TPC, D], F32, kind="ExternalOutput").ap()
    o_send = nc.dram_tensor("sendblk", [E * PAD, D + 8], BF16, kind="ExternalOutput").ap()
    o_counts = nc.dram_tensor("counts", [1, E], F32, kind="ExternalOutput").ap()
    o_meta = nc.dram_tensor("meta", [TPC, 4], F32, kind="ExternalOutput").ap()
    if DEBUG_LA:
        o_qown = nc.dram_tensor("dbg_qown", [128, DC * TPC], BF16, kind="ExternalOutput").ap()
        o_preln = nc.dram_tensor("dbg_preln", [128, QT * D], F32, kind="ExternalOutput").ap()
        o_sums = nc.dram_tensor("dbg_sums", [128, QT], F32, kind="ExternalOutput").ap()
        o_kfm = nc.dram_tensor("dbg_kfm", [128, DC * 1024], BF16, kind="ExternalOutput").ap()
        o_pbuf = nc.dram_tensor("dbg_pbuf", [128, KT * TPC], BF16, kind="ExternalOutput").ap()

    with ExitStack() as es:
        sb = lambda name, shape, dt: es.enter_context(nc.sbuf_tensor(name, shape, dt))
        wq_sb = sb("wq_sb", [128, DC, D], BF16)
        bq_sb = sb("bq_sb", [128, DC], F32)
        q_own = sb("q_own", [128, DC, TPC], BF16)
        x_grp = sb("x_grp", [128, 2, D], BF16)
        x_fm = sb("x_fm", [128, DC, 512], BF16)
        k_fm = sb("k_fm", [128, DC, 1024], BF16)
        v_grp = sb("v_grp", [128, KT, D], BF16)
        p_buf = sb("p_buf", [128, KT, TPC], BF16)
        out_acc = sb("out_acc", [128, QT, D], F32)
        sums_sb = sb("sums_sb", [128, QT], F32)
        xres = sb("xres", [128, 1, D], F32)
        id_bf = sb("id_bf", [128, 128], BF16)
        id_f32 = sb("id_f32", [128, 128], F32)
        ltri_sb = sb("ltri_sb", [128, 128], BF16)
        ones_cb = sb("ones_cb", [128, 1], BF16)
        ones_rf = sb("ones_rf", [1, 128], F32)
        wr_sb = sb("wr_sb", [128, DC, E], F32)
        x1fm = sb("x1fm", [128, 2, 128], F32)
        logit = sb("logit", [128, E], F32)
        iota8 = sb("iota8", [128, E], F32)
        eights = sb("eights", [128, E], F32)
        bigt = sb("bigt", [128, 1], F32)
        sc = {k: sb("sc_" + k, [128, 1], F32)
              for k in ("s", "s2", "m", "mm", "var", "std", "rstd", "rmax", "nmax",
                        "rsum", "pp", "eidx", "lrank", "sidxf", "sidxm")}
        sc["sq"] = sb("sc_sq", [128, D], F32)
        sc["eps"] = sb("sc_eps", [128, 1], F32)
        eqm = sb("eqm", [128, E], U8)
        idxm = sb("idxm", [128, E], F32)
        oh_f = sb("oh_f", [128, QT, E], F32)
        oh_b = sb("oh_b", [128, QT, E], BF16)
        csum = sb("csum", [128, QT, E], F32)
        obc = sb("obc", [128, 128], F32)
        trow = sb("trow", [1, 128], F32)
        orow = sb("orow", [1, 128], F32)
        orow2 = sb("orow2", [1, 128], F32)
        crow = sb("crow", [1, E], F32)
        meta_sb = sb("meta_sb", [128, QT, 4], F32)
        sidx_u = sb("sidx_u", [128, 1], U32)
        sstage = sb("sstage", [128, 1, D + 8], BF16)

        ps_pv = [es.enter_context(nc.psum_tensor(f"ps_pv{i}", [128, D], F32)) for i in range(2)]
        ps_st = [es.enter_context(nc.psum_tensor(f"ps_st{i}", [128, 512], F32)) for i in range(2)]
        ps_sum = es.enter_context(nc.psum_tensor("ps_sum", [128, 512], F32))
        ps_tr = es.enter_context(nc.psum_tensor("ps_tr", [128, 1024], BF16))

        p = P(nc, es)

        p.dma("sync", bq_sb[:], bq_pc)
        p.dma("sync", id_f32[:], c_identf)
        p.dma("sync", wr_sb[:], Wr.rearrange("(a p) e -> p a e", p=128))
        p.dma("sync", ones_rf[:], c_onesrf)
        p.dma("sync", iota8[:], c_iota8)
        p.dma("sync", eights[:], c_eights)
        p.dma("sync", bigt[:], c_big)
        p.dma("sync", id_bf[:], c_identb)
        p.dma("sync", ltri_sb[:], c_ltrib)
        p.dma("sync", ones_cb[:], c_onescb)
        p.dma("gp", wq_sb[:], Wq.rearrange("(a p) d -> p a d", p=128))
        lds_const = p.cnt["lds"]
        ldg_const = p.cnt["ldg"]
        p.instr("gp", lambda e: e.memset(sstage[:], 0.0))
        ng = p.instr("gp", lambda e: e.memset(sc["eps"][:], EPS))
        gp_const = ng

        def transpose_batch(tiles, evac_out, evac_in):
            p.guard("pe", "ps_tr")
            for i, (in_ap, slot) in enumerate(tiles):
                p.instr("pe", lambda e, a=in_ap, s=slot:
                        e.transpose(ps_tr[:, s * 128:(s + 1) * 128], a, id_bf[:]),
                        mark=(i == len(tiles) - 1))
            npe = p.cnt["pe"]
            p.wait("dve", "pe", npe)
            nd = p.instr("dve", lambda e, o=evac_out, i=evac_in: e.tensor_copy(o, i))
            p.use("ps_tr", "dve", nd)
            return nd

        def rows_to_xfm(rows_ap):
            for half in range(2):
                p.guard("gp", "x_grp")
                nld = p.dma("gp", x_grp[:],
                            rows_ap[half * 256:(half + 1) * 256, :]
                            .rearrange("(a p) d -> p a d", p=128))
                p.wait("pe", "ldg", nld)
                p.guard("pe", "x_fm")
                for tt in range(2):
                    gt = half * 2 + tt
                    tiles = [(x_grp[:, tt, j * 128:(j + 1) * 128], j) for j in range(DC)]
                    transpose_batch(tiles,
                                    x_fm[:, :, gt * 128:(gt + 1) * 128],
                                    ps_tr[:].rearrange("p (a n) -> p a n", a=DC))
                p.use("x_grp", "pe", p.cnt["pe"])

        def proj512(dst, off):
            ndv = p.cnt["dve"]
            p.wait("pe", "dve", ndv)
            for oc in range(DC):
                par = oc % 2
                pst = ps_st[par]
                p.guard("pe", f"ps_st{par}")
                for ic in range(DC):
                    p.instr("pe", lambda e, oc=oc, ic=ic, pst=pst:
                            e.matmul(pst[:, 0:512], wq_sb[:, ic, oc * 128:(oc + 1) * 128],
                                     x_fm[:, ic, :], start=(ic == 0), stop=(ic == DC - 1)),
                            mark=(ic == DC - 1))
                npe = p.cnt["pe"]
                p.wait("dve", "pe", npe)
                nd = p.instr("dve", lambda e, oc=oc, pst=pst, dst=dst, off=off:
                             e.tensor_scalar(dst[:, oc, off:off + 512], pst[:, 0:512],
                                             bq_sb[:, oc:oc + 1], None, ALU.add))
                p.use(f"ps_st{par}", "dve", nd)
            p.use("x_fm", "pe", p.cnt["pe"])

        p.wait("pe", "lds", lds_const)
        p.wait("pe", "ldg", ldg_const)
        p.wait("pe", "gp", gp_const)
        p.wait("dve", "lds", lds_const)
        p.wait("act", "lds", lds_const)
        for sub in range(4):
            rows_to_xfm(x_own[sub * 512:(sub + 1) * 512, :])
            proj512(q_own, sub * 512)

        if DEBUG_LA in (True, 1, 3):
            p.wait("sync", "pe", p.cnt["pe"])
            p.wait("sync", "dve", p.cnt["dve"])
            p.dma("sync", o_qown, q_own[:].rearrange("p a n -> p (a n)"))

        for g in range(KG):
            for sub in range(2):
                rows_to_xfm(x_batch[g * 1024 + sub * 512: g * 1024 + (sub + 1) * 512, :])
                p.guard("pe", "k_fm")
                proj512(k_fm, sub * 512)
            nk_dve = p.cnt["dve"]

            p.wait("pe", "dve", nk_dve)
            for kt in range(KT):
                tiles = [(k_fm[:, j, kt * 128:(kt + 1) * 128], j) for j in range(DC)]
                p.guard("pe", "ps_tr")
                p.guard("dve", "v_grp")
                transpose_batch(tiles, v_grp[:, kt, :], ps_tr[:])
            nv_dve = p.cnt["dve"]

            p.wait("pe", "dve", nv_dve)
            p.guard("act", "p_buf")
            for kt in range(KT):
                for qc in range(4):
                    par = (kt * 4 + qc) % 2
                    p.guard("pe", f"ps_st{par}")
                    for ic in range(DC):
                        p.instr("pe", lambda e, kt=kt, qc=qc, ic=ic, par=par:
                                e.matmul(ps_st[par][:, 0:512],
                                         k_fm[:, ic, kt * 128:(kt + 1) * 128],
                                         q_own[:, ic, qc * 512:(qc + 1) * 512],
                                         start=(ic == 0), stop=(ic == DC - 1)),
                                mark=(ic == DC - 1))
                    npe = p.cnt["pe"]
                    p.wait("act", "pe", npe)
                    na = p.instr("act", lambda e, kt=kt, qc=qc, par=par:
                                 e.activation(p_buf[:, kt, qc * 512:(qc + 1) * 512],
                                              ps_st[par][:, 0:512], AF.Exp,
                                              bias=0.0, scale=SCALE))
                    p.use(f"ps_st{par}", "act", na)
            nexp = p.cnt["act"]

            p.wait("pe", "act", nexp)
            for qt in range(QT):
                par = qt % 2
                p.guard("pe", f"ps_pv{par}")
                for kt in range(KT):
                    lhs = p_buf[:, kt, qt * 128:(qt + 1) * 128]
                    p.instr("pe", lambda e, lhs=lhs, kt=kt, par=par:
                            e.matmul(ps_pv[par][:, 0:512], lhs, v_grp[:, kt, 0:512],
                                     start=(kt == 0), stop=(kt == KT - 1)), mark=False)
                    p.instr("pe", lambda e, lhs=lhs, kt=kt, par=par:
                            e.matmul(ps_pv[par][:, 512:1024], lhs, v_grp[:, kt, 512:1024],
                                     start=(kt == 0), stop=(kt == KT - 1)), mark=False)
                    p.instr("pe", lambda e, lhs=lhs, kt=kt, qt=qt, g=g:
                            e.matmul(ps_sum[:, g * QT + qt: g * QT + qt + 1], lhs, ones_cb[:],
                                     start=(kt == 0), stop=(kt == KT - 1)),
                            mark=(kt == KT - 1))
                npe = p.cnt["pe"]
                p.wait("dve", "pe", npe)
                if g == 0:
                    nd = p.instr("dve", lambda e, qt=qt, par=par:
                                 e.tensor_copy(out_acc[:, qt, :], ps_pv[par][:, 0:1024]))
                else:
                    nd = p.instr("dve", lambda e, qt=qt, par=par:
                                 e.tensor_tensor(out_acc[:, qt, :], out_acc[:, qt, :],
                                                 ps_pv[par][:, 0:1024], ALU.add))
                p.use(f"ps_pv{par}", "dve", nd)
            p.use("p_buf", "pe", p.cnt["pe"])
            p.use("v_grp", "pe", p.cnt["pe"])
            p.use("k_fm", "pe", p.cnt["pe"])

        npe_all = p.cnt["pe"]
        p.wait("dve", "pe", npe_all)
        nd = p.instr("dve", lambda e: e.tensor_reduce(
            sums_sb[:].rearrange("p (q o) -> p q o", o=1),
            ps_sum[:, 0:128].rearrange("p (g q) -> p q g", g=KG), AX.X, ALU.add))
        p.use("ps_sum", "dve", nd)
        if DEBUG_LA in (True, 2, 3):
            p.wait("sync", "dve", p.cnt["dve"])
            p.dma("sync", o_preln, out_acc[:].rearrange("p a n -> p (a n)"))
            p.dma("sync", o_sums, sums_sb[:])
            p.dma("sync", o_kfm, k_fm[:].rearrange("p a n -> p (a n)"))
            p.dma("sync", o_pbuf, p_buf[:].rearrange("p a n -> p (a n)"))
        p.wait("act", "gp", gp_const)
        for qt in range(QT):
            p.guard("sync", "xres0")
            nld = p.dma("sync", xres[:, 0, :],
                        x_own[qt * 128:(qt + 1) * 128, :])
            p.chain("dve")
            p.instr("dve", lambda e, qt=qt: e.reciprocal(sc["rstd"][:], sums_sb[:, qt:qt + 1]))
            p.wait("dve", "lds", nld)
            p.chain("dve")
            nr = p.instr("dve", lambda e, qt=qt: e.scalar_tensor_tensor(
                out_acc[:, qt, :], out_acc[:, qt, :], sc["rstd"][:], xres[:, 0, :],
                ALU.mult, ALU.add))
            p.use("xres0", "dve", nr)
            emit_ln_tile(p, out_acc[:, qt, :], sc)
        nx1 = p.cnt["dve"]
        p.wait("sync", "dve", nx1)
        p.dma("sync", o_x1.rearrange("(t p) d -> p t d", p=128), out_acc[:, :, :])

        p.wait("pe", "dve", nx1)
        for qt in range(QT):
            p.guard("pe", "ps_st0")
            for j in range(DC):
                p.guard("pe", "ps_st1")
                ntr = p.instr("pe", lambda e, qt=qt, j=j: e.transpose(
                    ps_st[1][:, 0:128],
                    out_acc[:, qt, j * 128:(j + 1) * 128], id_f32[:]))
                p.wait("dve", "pe", ntr)
                p.guard("dve", f"x1fm{j % 2}")
                ndv = p.instr("dve", lambda e, j=j: e.tensor_copy(
                    x1fm[:, j % 2, :], ps_st[1][:, 0:128]))
                p.use("ps_st1", "dve", ndv)
                p.wait("pe", "dve", ndv)
                p.instr("pe", lambda e, j=j: e.matmul(
                    ps_st[0][:, 0:E], x1fm[:, j % 2, :], wr_sb[:, j, :],
                    start=(j == 0), stop=(j == DC - 1)),
                    mark=(j == DC - 1))
                p.use(f"x1fm{j % 2}", "pe", p.cnt["pe"])
            npe = p.cnt["pe"]
            p.wait("dve", "pe", npe)
            p.chain("dve")
            nl = p.instr("dve", lambda e: e.tensor_copy(logit[:], ps_st[0][:, 0:E]))
            p.use("ps_st0", "dve", nl)
            p.chain("dve")
            p.instr("dve", lambda e: e.tensor_reduce(sc["rmax"][:], logit[:], AX.X, ALU.max))
            p.chain("dve")
            nm = p.instr("dve", lambda e: e.tensor_scalar(sc["nmax"][:], sc["rmax"][:], -1.0, None, ALU.mult))
            p.wait("act", "dve", nm)
            p.guard("act", "sc_sq")
            na = p.instr("act", lambda e: e.activation(sc["sq"][:, 0:E], logit[:], AF.Exp,
                                                       bias=sc["nmax"][:], scale=1.0))
            p.wait("dve", "act", na)
            p.instr("dve", lambda e: e.tensor_reduce(sc["rsum"][:], sc["sq"][:, 0:E], AX.X, ALU.add))
            p.use("sc_sq", "dve", p.cnt["dve"])
            p.chain("dve")
            p.instr("dve", lambda e: e.reciprocal(sc["pp"][:], sc["rsum"][:]))
            p.instr("dve", lambda e: e.tensor_scalar(eqm[:], logit[:], sc["rmax"][:], None, ALU.is_ge))
            p.chain("dve")
            p.instr("dve", lambda e: e.select(idxm[:], eqm[:], iota8[:], eights[:]))
            p.chain("dve")
            p.instr("dve", lambda e: e.tensor_reduce(sc["eidx"][:], idxm[:], AX.X, ALU.min))
            p.chain("dve")
            p.instr("dve", lambda e, qt=qt: e.tensor_scalar(
                oh_f[:, qt, :], iota8[:], sc["eidx"][:], None, ALU.is_equal))
            p.chain("dve")
            p.instr("dve", lambda e, qt=qt: e.tensor_copy(oh_b[:, qt, :], oh_f[:, qt, :]))
            p.instr("dve", lambda e, qt=qt: e.tensor_copy(meta_sb[:, qt, 0:1], sc["eidx"][:]))
            p.instr("dve", lambda e, qt=qt: e.tensor_copy(meta_sb[:, qt, 1:2], sc["pp"][:]))

        noh = p.cnt["dve"]
        p.wait("pe", "dve", noh)
        for qt in range(QT):
            p.guard("pe", "ps_st1")
            npq = p.instr("pe", lambda e, qt=qt: e.matmul(
                ps_st[1][:, 0:E], ltri_sb[:], oh_b[:, qt, :], start=True, stop=True))
            p.wait("dve", "pe", npq)
            ncs = p.instr("dve", lambda e, qt=qt: e.tensor_tensor(
                csum[:, qt, :], ps_st[1][:, 0:E], oh_f[:, qt, :], ALU.subtract))
            p.use("ps_st1", "dve", ncs)
        p.guard("pe", "ps_st0")
        ntt = p.instr("pe", lambda e: e.matmul(
            ps_st[0][0:1, 0:128], ones_cb[:],
            oh_b[:, :, :].rearrange("p a e -> p (a e)"), start=True, stop=True))
        p.wait("dve", "pe", ntt)
        p.chain("dve")
        ntr2 = p.instr("dve", lambda e: e.tensor_copy(trow[:], ps_st[0][0:1, 0:128]))
        p.use("ps_st0", "dve", ntr2)
        p.chain("dve")
        p.instr("dve", lambda e: e.tensor_copy(orow[:], trow[:]))
        cur, other = orow, orow2
        for step in (8, 16, 32, 64):
            c0, o0 = cur, other
            p.chain("dve")
            p.instr("dve", lambda e, c0=c0, o0=o0, s=step: e.tensor_copy(o0[:, 0:s], c0[:, 0:s]))
            p.instr("dve", lambda e, c0=c0, o0=o0, s=step: e.tensor_tensor(
                o0[:, s:128], c0[:, s:128], c0[:, 0:128 - s], ALU.add))
            cur, other = other, cur
        p.chain("dve")
        nexc = p.instr("dve", lambda e, c0=cur: e.tensor_tensor(c0[:], c0[:], trow[:], ALU.subtract))
        p.chain("dve")
        ncr = p.instr("dve", lambda e, c0=cur: e.tensor_tensor(
            crow[:], c0[:, 120:128], trow[:, 120:128], ALU.add))
        p.wait("sync", "dve", ncr)
        p.dma("sync", o_counts, crow[:])
        p.wait("pe", "dve", nexc)
        p.guard("pe", "ps_st0")
        nbc = p.instr("pe", lambda e, c0=cur: e.matmul(
            ps_st[0][:, 0:128], ones_rf[:], c0[:], start=True, stop=True))
        p.wait("dve", "pe", nbc)
        nob = p.instr("dve", lambda e: e.tensor_copy(obc[:], ps_st[0][:, 0:128]))
        p.use("ps_st0", "dve", nob)

        p.wait("act", "dve", nx1)
        obc3 = obc[:].rearrange("p (a e) -> p a e", e=E)
        for qt in range(QT):
            p.chain("dve")
            p.instr("dve", lambda e, qt=qt: e.tensor_tensor(
                idxm[:], csum[:, qt, :], obc3[:, qt, :], ALU.add))
            p.chain("dve")
            p.instr("dve", lambda e, qt=qt: e.tensor_tensor(
                idxm[:], idxm[:], oh_f[:, qt, :], ALU.mult))
            p.chain("dve")
            p.instr("dve", lambda e: e.tensor_reduce(sc["lrank"][:], idxm[:], AX.X, ALU.add))
            p.chain("dve")
            p.instr("dve", lambda e, qt=qt: e.tensor_copy(meta_sb[:, qt, 2:3], sc["lrank"][:]))
            p.instr("dve", lambda e, qt=qt: e.scalar_tensor_tensor(
                sc["sidxf"][:], meta_sb[:, qt, 0:1], float(PAD), sc["lrank"][:],
                ALU.mult, ALU.add))
            p.instr("dve", lambda e: e.tensor_scalar(
                eqm[:, 0:1], sc["lrank"][:], float(PAD) - 0.5, None, ALU.is_lt))
            p.chain("dve")
            p.instr("dve", lambda e: e.select(sc["sidxm"][:], eqm[:, 0:1], sc["sidxf"][:], bigt[:]))
            p.guard("dve", "sidx_u")
            p.chain("dve")
            nsi = p.instr("dve", lambda e: e.tensor_copy(sidx_u[:], sc["sidxm"][:]))
            p.guard("act", "sstage0")
            nst1 = p.instr("act", lambda e, qt=qt: e.activation(
                sstage[:, 0, 0:D], out_acc[:, qt, :], AF.Copy, bias=0.0, scale=1.0))
            p.guard("dve", "sstage0")
            nst2 = p.instr("dve", lambda e, qt=qt: e.tensor_copy(
                sstage[:, 0, D:D + 1], meta_sb[:, qt, 1:2]))
            p.wait("gp", "dve", max(nsi, nst2))
            p.wait("gp", "act", nst1)
            p.cnt["gp"] += 16
            sgp = p.sem["gp"]
            p.emit("gp", lambda e, qt=qt, sgp=sgp: e.indirect_dma_start(
                o_send, IndirectOffsetOnAxis(sidx_u[:], 0),
                sstage[:, 0, :], None,
                bounds_check=E * PAD - 1, oob_is_err=False).then_inc(sgp, 16))
            ngp = p.cnt["gp"]
            p.use("sstage0", "gp", ngp)
            p.use("sidx_u", "gp", ngp)

        nmeta_all = p.cnt["dve"]
        p.wait("sync", "dve", nmeta_all)
        p.dma("sync", o_meta.rearrange("(t p) m -> p t m", p=128), meta_sb[:, :, :])

        p.wait("sync", "lds", p.cnt["lds"])
        p.wait("sync", "ldg", p.cnt["ldg"])
        p.wait("sync", "gp", p.cnt["gp"])

        block = es.enter_context(nc.Block())
        p.build(block)
    return nc


def build_lb():
    nc = bass.Bass("TRN2", target_bir_lowering=False, debug=False)
    recv = nc.dram_tensor("recv", [NCORES * PAD, D + 8], BF16, kind="ExternalInput").ap()
    counts = nc.dram_tensor("counts", [NCORES, E], F32, kind="ExternalInput").ap()
    sel_bc = nc.dram_tensor("sel_bc", [E, E], F32, kind="ExternalInput").ap()
    W1 = nc.dram_tensor("W1", [D, FF], F32, kind="ExternalInput").ap()
    W2 = nc.dram_tensor("W2", [FF, D], F32, kind="ExternalInput").ap()
    c_identf = nc.dram_tensor("c_identf", [128, 128], F32, kind="ExternalInput").ap()
    c_identb = nc.dram_tensor("c_identb", [128, 128], BF16, kind="ExternalInput").ap()
    c_ltri8s = nc.dram_tensor("c_ltri8s", [E, E], F32, kind="ExternalInput").ap()
    c_ones8 = nc.dram_tensor("c_ones8", [E, 1], F32, kind="ExternalInput").ap()
    c_onesrf = nc.dram_tensor("c_onesrf", [1, 128], F32, kind="ExternalInput").ap()
    c_iota8 = nc.dram_tensor("c_iota8", [128, E], F32, kind="ExternalInput").ap()
    c_iota128 = nc.dram_tensor("c_iota128", [128, 1], F32, kind="ExternalInput").ap()
    c_big = nc.dram_tensor("c_big", [128, 1], F32, kind="ExternalInput").ap()
    o_ret = nc.dram_tensor("retblk", [NCORES * PAD, D], BF16, kind="ExternalOutput").ap()

    with ExitStack() as es:
        sb = lambda name, shape, dt: es.enter_context(nc.sbuf_tensor(name, shape, dt))
        w1_sb = sb("w1_sb", [128, DC, FF], BF16)
        w2_sb = sb("w2_sb", [128, 2, 4, D], BF16)
        xe = sb("xe", [128, QT, D + 8], BF16)
        xe_fm = sb("xe_fm", [128, DC, CAP], BF16)
        pe_col = sb("pe_col", [128, QT], F32)
        h_fm = sb("h_fm", [128, FC, 512], BF16)
        y_acc = sb("y_acc", [128, 4, D], F32)
        ystage = sb("ystage", [128, 2, D], BF16)
        id_bf = sb("id_bf", [128, 128], BF16)
        id_f32 = sb("id_f32", [128, 128], F32)
        cnt_sb = sb("cnt_sb", [E, E], F32)
        selbc = sb("selbc", [E, E], F32)
        ltri8f = sb("ltri8f", [E, E], F32)
        ones8 = sb("ones8", [E, 1], F32)
        ones_rf = sb("ones_rf", [1, 128], F32)
        iota8 = sb("iota8", [128, E], F32)
        iota128 = sb("iota128", [128, 1], F32)
        bigt = sb("bigt", [128, 1], F32)
        cmine = sb("cmine", [E, E], F32)
        ccol = sb("ccol", [E, 1], F32)
        corow = sb("corow", [1, E], F32)
        totv = sb("totv", [1, 1], F32)
        cbc = sb("cbc", [128, E], F32)
        nvbc = sb("nvbc", [128, 1], F32)
        stile = sb("stile", [128, 1], F32)
        csel = sb("csel", [128, E], F32)
        cselq = sb("cselq", [128, E], F32)
        cno = sb("cno", [128, 1], F32)
        coffsel = sb("coffsel", [128, 1], F32)
        srcf = sb("srcf", [128, 1], F32)
        gm = sb("gm", [128, 1], U8)
        srcm = sb("srcm", [128, 1], F32)
        gsrc_all = sb("gsrc_all", [128, QT], U32)

        ps_h = [es.enter_context(nc.psum_tensor(f"ps_h{i}", [128, 512], F32)) for i in range(2)]
        ps_y = [es.enter_context(nc.psum_tensor(f"ps_y{i}", [128, D], F32)) for i in range(2)]
        ps_tr = es.enter_context(nc.psum_tensor("ps_tr2", [128, 1024], BF16))
        ps_s = es.enter_context(nc.psum_tensor("ps_s", [128, 128], F32))

        p = P(nc, es)

        p.dma("sync", cnt_sb[:], counts)
        p.dma("sync", selbc[:], sel_bc)
        p.dma("sync", ltri8f[:], c_ltri8s)
        p.dma("sync", ones8[:], c_ones8)
        p.dma("sync", ones_rf[:], c_onesrf)
        p.dma("sync", iota8[:], c_iota8)
        p.dma("sync", iota128[:], c_iota128)
        p.dma("sync", bigt[:], c_big)
        p.dma("sync", id_f32[:], c_identf)
        p.dma("sync", id_bf[:], c_identb)
        ld_c = p.cnt["lds"]
        nxz = p.instr("gp", lambda e: e.memset(xe[:], 0.0))
        nw1 = p.dma("gp", w1_sb[:], W1.rearrange("(a p) f -> p a f", p=128))

        p.wait("dve", "lds", ld_c)
        p.instr("dve", lambda e: e.tensor_tensor(cmine[:], cnt_sb[:], selbc[:], ALU.mult))
        p.chain("dve")
        ncm = p.instr("dve", lambda e: e.tensor_reduce(ccol[:], cmine[:], AX.X, ALU.add))
        p.wait("pe", "dve", ncm)
        p.wait("pe", "lds", ld_c)
        n1 = p.instr("pe", lambda e: e.matmul(ps_s[0:E, 0:1], ltri8f[:], ccol[:],
                                              start=True, stop=True))
        n2 = p.instr("pe", lambda e: e.matmul(ps_s[0:1, 4:5], ones8[:], ccol[:],
                                              start=True, stop=True))
        p.wait("dve", "pe", n2)
        nco = p.instr("dve", lambda e: e.tensor_copy(ccol[:], ps_s[0:E, 0:1]))
        p.chain("dve")
        ntv = p.instr("dve", lambda e: e.tensor_copy(totv[:], ps_s[0:1, 4:5]))
        p.use("ps_s", "dve", ntv)
        p.wait("pe", "dve", ntv)
        p.guard("pe", "ps_s")
        n3 = p.instr("pe", lambda e: e.transpose(ps_s[0:1, 0:E], ccol[:], id_f32[0:E, 0:E]))
        p.wait("dve", "pe", n3)
        p.chain("dve")
        nrow = p.instr("dve", lambda e: e.tensor_copy(corow[:], ps_s[0:1, 0:E]))
        p.use("ps_s", "dve", nrow)
        p.chain("dve")
        nnv = p.instr("dve", lambda e: e.tensor_scalar(totv[:], totv[:], float(CAP), None, ALU.min))
        p.wait("pe", "dve", max(nrow, nnv))
        p.guard("pe", "ps_s")
        n4 = p.instr("pe", lambda e: e.matmul(ps_s[:, 0:E], ones_rf[:], corow[:],
                                              start=True, stop=True))
        n5 = p.instr("pe", lambda e: e.matmul(ps_s[:, 8:9], ones_rf[:], totv[:],
                                              start=True, stop=True))
        p.wait("dve", "pe", n5)
        p.chain("dve")
        p.instr("dve", lambda e: e.tensor_copy(cbc[:], ps_s[:, 0:E]))
        nbc = p.instr("dve", lambda e: e.tensor_copy(nvbc[:], ps_s[:, 8:9]))
        p.use("ps_s", "dve", nbc)

        for st in range(QT):
            p.chain("dve")
            p.instr("dve", lambda e, st=st: e.tensor_scalar(
                stile[:], iota128[:], float(st * 128), None, ALU.add))
            p.chain("dve")
            p.instr("dve", lambda e: e.tensor_scalar(csel[:], cbc[:], stile[:], None, ALU.is_le))
            p.chain("dve")
            p.instr("dve", lambda e: e.tensor_reduce(cno[:], csel[:], AX.X, ALU.add))
            p.chain("dve")
            p.instr("dve", lambda e: e.tensor_scalar(cno[:], cno[:], -1.0, None, ALU.add))
            p.chain("dve")
            p.instr("dve", lambda e: e.tensor_scalar(cselq[:], iota8[:], cno[:], None, ALU.is_equal))
            p.chain("dve")
            p.instr("dve", lambda e: e.tensor_tensor(cselq[:], cselq[:], cbc[:], ALU.mult))
            p.chain("dve")
            p.instr("dve", lambda e: e.tensor_reduce(coffsel[:], cselq[:], AX.X, ALU.add))
            p.chain("dve")
            p.instr("dve", lambda e: e.tensor_tensor(srcf[:], stile[:], coffsel[:], ALU.subtract))
            p.chain("dve")
            p.instr("dve", lambda e: e.scalar_tensor_tensor(
                srcf[:], cno[:], float(PAD), srcf[:], ALU.mult, ALU.add))
            p.instr("dve", lambda e: e.tensor_scalar(gm[:], stile[:], nvbc[:], None, ALU.is_lt))
            p.chain("dve")
            p.instr("dve", lambda e: e.select(srcm[:], gm[:], srcf[:], bigt[:]))
            p.chain("dve")
            nsi = p.instr("dve", lambda e, st=st: e.tensor_copy(gsrc_all[:, st:st + 1], srcm[:]))
            p.wait("gp", "dve", nsi)
            p.cnt["gp"] += 16
            sgp = p.sem["gp"]
            p.emit("gp", lambda e, st=st, sgp=sgp: e.indirect_dma_start(
                xe[:, st, :], None, recv, IndirectOffsetOnAxis(gsrc_all[:, st:st + 1], 0),
                bounds_check=NCORES * PAD - 1, oob_is_err=False).then_inc(sgp, 16))
        ngather = p.cnt["gp"]

        p.wait("pe", "gp", ngather)
        p.wait("dve", "gp", ngather)
        for st in range(QT):
            p.guard("pe", "ps_tr2x")
            for j in range(DC):
                p.instr("pe", lambda e, st=st, j=j: e.transpose(
                    ps_tr[:, j * 128:(j + 1) * 128], xe[:, st, j * 128:(j + 1) * 128],
                    id_bf[:]), mark=(j == DC - 1))
            npe = p.cnt["pe"]
            p.wait("dve", "pe", npe)
            nd = p.instr("dve", lambda e, st=st: e.tensor_copy(
                xe_fm[:, :, st * 128:(st + 1) * 128],
                ps_tr[:].rearrange("p (a n) -> p a n", a=DC)))
            p.use("ps_tr2x", "dve", nd)
            p.instr("dve", lambda e, st=st: e.tensor_copy(
                pe_col[:, st:st + 1], xe[:, st, D:D + 1]))
        nxf = p.cnt["dve"]

        p.wait("pe", "dve", nxf)
        p.wait("pe", "ldg", nw1)
        for s4 in range(4):
            for fc in range(FC):
                par = fc % 2
                p.guard("pe", f"ps_h{par}")
                for ic in range(DC):
                    p.instr("pe", lambda e, fc=fc, ic=ic, s4=s4, par=par:
                            e.matmul(ps_h[par][:, 0:512],
                                     w1_sb[:, ic, fc * 128:(fc + 1) * 128],
                                     xe_fm[:, ic, s4 * 512:(s4 + 1) * 512],
                                     start=(ic == 0), stop=(ic == DC - 1)),
                            mark=(ic == DC - 1))
                npe = p.cnt["pe"]
                p.wait("act", "pe", npe)
                p.guard("act", f"h_fm{fc}")
                na = p.instr("act", lambda e, fc=fc, par=par: e.activation(
                    h_fm[:, fc, :], ps_h[par][:, 0:512], AF.Relu, bias=0.0, scale=1.0))
                p.use(f"ps_h{par}", "act", na)
            nh = p.cnt["act"]
            p.wait("pe", "act", nh)
            for fg in range(8):
                buf = fg % 2
                p.guard("gp", f"w2_{buf}")
                nw2 = p.dma("gp", w2_sb[:, buf, :, :],
                            W2[fg * 512:(fg + 1) * 512, :].rearrange("(a p) d -> p a d", p=128))
                p.wait("pe", "ldg", nw2)
                for tl in range(4):
                    par = tl % 2
                    p.guard("pe", f"ps_y{par}")
                    for f4 in range(4):
                        fc = fg * 4 + f4
                        lhs = h_fm[:, fc, tl * 128:(tl + 1) * 128]
                        p.instr("pe", lambda e, lhs=lhs, f4=f4, buf=buf, par=par:
                                e.matmul(ps_y[par][:, 0:512], lhs, w2_sb[:, buf, f4, 0:512],
                                         start=(f4 == 0), stop=(f4 == 3)), mark=False)
                        p.instr("pe", lambda e, lhs=lhs, f4=f4, buf=buf, par=par:
                                e.matmul(ps_y[par][:, 512:1024], lhs, w2_sb[:, buf, f4, 512:1024],
                                         start=(f4 == 0), stop=(f4 == 3)), mark=(f4 == 3))
                    npe = p.cnt["pe"]
                    p.wait("dve", "pe", npe)
                    if fg == 0:
                        nd = p.instr("dve", lambda e, tl=tl, par=par:
                                     e.tensor_copy(y_acc[:, tl, :], ps_y[par][:, 0:1024]))
                    else:
                        nd = p.instr("dve", lambda e, tl=tl, par=par:
                                     e.tensor_tensor(y_acc[:, tl, :], y_acc[:, tl, :],
                                                     ps_y[par][:, 0:1024], ALU.add))
                    p.use(f"ps_y{par}", "dve", nd)
                p.use(f"w2_{buf}", "pe", p.cnt["pe"])
            ny = p.cnt["dve"]
            for tl in range(4):
                st = s4 * 4 + tl
                p.wait("act", "dve", ny)
                p.guard("act", f"ystage{tl % 2}")
                na = p.instr("act", lambda e, tl=tl, st=st: e.activation(
                    ystage[:, tl % 2, :], y_acc[:, tl, :], AF.Copy, bias=0.0,
                    scale=pe_col[:, st:st + 1]))
                p.wait("gp", "act", na)
                p.cnt["gp"] += 16
                sgp = p.sem["gp"]
                p.emit("gp", lambda e, tl=tl, st=st, sgp=sgp: e.indirect_dma_start(
                    o_ret, IndirectOffsetOnAxis(gsrc_all[:, st:st + 1], 0),
                    ystage[:, tl % 2, :], None,
                    bounds_check=NCORES * PAD - 1, oob_is_err=False).then_inc(sgp, 16))
                ngp = p.cnt["gp"]
                p.use(f"ystage{tl % 2}", "gp", ngp)
            p.use("y_acc", "dve", ny)
            p.res["y_acc"] = ("gp", p.cnt["gp"])
            p.guard("dve", "y_acc")

        p.wait("sync", "lds", p.cnt["lds"])
        p.wait("sync", "ldg", p.cnt["ldg"])
        p.wait("sync", "gp", p.cnt["gp"])
        p.wait("sync", "dve", p.cnt["dve"])
        p.wait("sync", "act", p.cnt["act"])

        block = es.enter_context(nc.Block())
        p.build(block)
    return nc


def build_lc():
    nc = bass.Bass("TRN2", target_bir_lowering=False, debug=False)
    x1 = nc.dram_tensor("x1", [TPC, D], F32, kind="ExternalInput").ap()
    retg = nc.dram_tensor("retg", [E * PAD, D], BF16, kind="ExternalInput").ap()
    meta = nc.dram_tensor("meta", [TPC, 4], F32, kind="ExternalInput").ap()
    counts = nc.dram_tensor("counts", [NCORES, E], F32, kind="ExternalInput").ap()
    mask_lt = nc.dram_tensor("mask_lt", [NCORES, 1], F32, kind="ExternalInput").ap()
    c_ones8 = nc.dram_tensor("c_ones8", [E, 1], F32, kind="ExternalInput").ap()
    c_onesrf = nc.dram_tensor("c_onesrf", [1, 128], F32, kind="ExternalInput").ap()
    c_iota8 = nc.dram_tensor("c_iota8", [128, E], F32, kind="ExternalInput").ap()
    c_big = nc.dram_tensor("c_big", [128, 1], F32, kind="ExternalInput").ap()
    o_out = nc.dram_tensor("out", [TPC, D], F32, kind="ExternalOutput").ap()

    with ExitStack() as es:
        sb = lambda name, shape, dt: es.enter_context(nc.sbuf_tensor(name, shape, dt))
        x1_sb = sb("x1_sb", [128, QT, D], F32)
        meta_sb = sb("meta_sb", [128, QT, 4], F32)
        ybuf = sb("ybuf", [128, 2, D], BF16)
        hbuf = sb("hbuf", [128, 2, D], F32)
        cnt_sb = sb("cnt_sb", [E, E], F32)
        mlt = sb("mlt", [E, 1], F32)
        cmask = sb("cmask", [E, E], F32)
        corow = sb("corow", [1, E], F32)
        cbc = sb("cbc", [128, E], F32)
        iota8 = sb("iota8", [128, E], F32)
        bigt = sb("bigt", [128, 1], F32)
        ones8 = sb("ones8", [E, 1], F32)
        ones_rf = sb("ones_rf", [1, 128], F32)
        ohx = sb("ohx", [128, E], F32)
        coffsel = sb("coffsel", [128, 1], F32)
        grank = sb("grank", [128, 1], F32)
        km = sb("km", [128, 1], U8)
        gsf = sb("gsf", [128, 1], F32)
        gsm = sb("gsm", [128, 1], F32)
        gsu = sb("gsu", [128, 1], U32)
        sc = {k: sb("sc_" + k, [128, 1], F32)
              for k in ("s", "s2", "m", "mm", "var", "std", "rstd")}
        sc["sq"] = sb("sc_sq", [128, D], F32)
        sc["eps"] = sb("sc_eps", [128, 1], F32)

        ps_s = es.enter_context(nc.psum_tensor("ps_sc", [128, 128], F32))

        p = P(nc, es)
        p.instr("gp", lambda e: e.memset(sc["eps"][:], EPS))
        p.dma("sync", cnt_sb[:], counts)
        p.dma("sync", mlt[:], mask_lt)
        p.dma("sync", ones8[:], c_ones8)
        p.dma("sync", ones_rf[:], c_onesrf)
        p.dma("sync", iota8[:], c_iota8)
        p.dma("sync", bigt[:], c_big)
        p.dma("sync", meta_sb[:], meta.rearrange("(t p) m -> p t m", p=128))
        p.dma("sync", x1_sb[:], x1.rearrange("(t p) d -> p t d", p=128))
        ld_c = p.cnt["lds"]

        p.wait("act", "gp", 1)
        p.wait("dve", "lds", ld_c)
        p.instr("dve", lambda e: e.tensor_scalar(cmask[:], cnt_sb[:], mlt[:], None, ALU.mult))
        ncm = p.cnt["dve"]
        p.wait("pe", "dve", ncm)
        p.wait("pe", "lds", ld_c)
        n1 = p.instr("pe", lambda e: e.matmul(ps_s[0:1, 0:E], ones8[:], cmask[:],
                                              start=True, stop=True))
        p.wait("dve", "pe", n1)
        p.chain("dve")
        nro = p.instr("dve", lambda e: e.tensor_copy(corow[:], ps_s[0:1, 0:E]))
        p.use("ps_sc", "dve", nro)
        p.wait("pe", "dve", nro)
        p.guard("pe", "ps_sc")
        n2 = p.instr("pe", lambda e: e.matmul(ps_s[:, 0:E], ones_rf[:], corow[:],
                                              start=True, stop=True))
        p.wait("dve", "pe", n2)
        p.chain("dve")
        nbc = p.instr("dve", lambda e: e.tensor_copy(cbc[:], ps_s[:, 0:E]))
        p.use("ps_sc", "dve", nbc)

        for qt in range(QT):
            par = qt % 2
            p.chain("dve")
            p.instr("dve", lambda e, qt=qt: e.tensor_scalar(
                ohx[:], iota8[:], meta_sb[:, qt, 0:1], None, ALU.is_equal))
            p.chain("dve")
            p.instr("dve", lambda e: e.tensor_tensor(ohx[:], ohx[:], cbc[:], ALU.mult))
            p.chain("dve")
            p.instr("dve", lambda e: e.tensor_reduce(coffsel[:], ohx[:], AX.X, ALU.add))
            p.chain("dve")
            p.instr("dve", lambda e, qt=qt: e.tensor_tensor(
                grank[:], coffsel[:], meta_sb[:, qt, 2:3], ALU.add))
            p.chain("dve")
            p.instr("dve", lambda e: e.tensor_scalar(km[:], grank[:], float(CAP) - 0.5, None, ALU.is_lt))
            p.instr("dve", lambda e, qt=qt: e.scalar_tensor_tensor(
                gsf[:], meta_sb[:, qt, 0:1], float(PAD), meta_sb[:, qt, 2:3],
                ALU.mult, ALU.add))
            p.chain("dve")
            p.instr("dve", lambda e: e.select(gsm[:], km[:], gsf[:], bigt[:]))
            p.guard("dve", "gsu")
            p.chain("dve")
            nsi = p.instr("dve", lambda e: e.tensor_copy(gsu[:], gsm[:]))
            p.guard("gp", f"ybuf{par}")
            ngz = p.instr("gp", lambda e, par=par: e.memset(ybuf[:, par, :], 0.0))
            p.wait("gp", "dve", nsi)
            p.cnt["gp"] += 16
            sgp = p.sem["gp"]
            p.emit("gp", lambda e, par=par, sgp=sgp: e.indirect_dma_start(
                ybuf[:, par, :], None, retg, IndirectOffsetOnAxis(gsu[:], 0),
                bounds_check=E * PAD - 1, oob_is_err=False).then_inc(sgp, 16))
            ngp = p.cnt["gp"]
            p.use("gsu", "gp", ngp)
            p.wait("dve", "gp", ngp)
            p.guard("dve", f"hbuf{par}")
            p.instr("dve", lambda e, qt=qt, par=par: e.tensor_tensor(
                hbuf[:, par, :], x1_sb[:, qt, :], ybuf[:, par, :], ALU.add))
            p.use(f"ybuf{par}", "dve", p.cnt["dve"])
            nf = emit_ln_tile(p, hbuf[:, par, :], sc)
            p.wait("sync", "dve", nf)
            p.dma("sync", o_out[qt * 128:(qt + 1) * 128, :], hbuf[:, par, :])
            p.use(f"hbuf{par}", "lds", p.cnt["lds"])

        p.wait("sync", "lds", p.cnt["lds"])
        p.wait("sync", "gp", p.cnt["gp"])

        block = es.enter_context(nc.Block())
        p.build(block)
    return nc


_NC_CACHE = {}


def _consts():
    c = {}
    c["c_identf"] = np.eye(128, dtype=np.float32)
    c["c_identb"] = np.eye(128, dtype=np.float32).astype(BFD)
    lt = (np.arange(128)[:, None] <= np.arange(128)[None, :]).astype(np.float32)
    c["c_ltrib"] = lt.astype(BFD)
    c["c_onescb"] = np.ones((128, 1), np.float32).astype(BFD)
    c["c_onesrf"] = np.ones((1, 128), np.float32)
    c["c_iota8"] = np.tile(np.arange(E, dtype=np.float32)[None, :], (128, 1))
    c["c_eights"] = np.full((128, E), 8.0, np.float32)
    c["c_big"] = np.full((128, 1), BIG, np.float32)
    c["c_ltri8s"] = (np.arange(E)[:, None] < np.arange(E)[None, :]).astype(np.float32)
    c["c_ones8"] = np.ones((E, 1), np.float32)
    c["c_iota128"] = np.arange(128, dtype=np.float32).reshape(128, 1)
    return c


def kernel(L0_attn_inputs, L0_attn_Wq, L0_attn_bq, L0_ffn_Wr, L0_ffn_W1, L0_ffn_W2):
    x = np.ascontiguousarray(np.asarray(L0_attn_inputs, np.float32))
    Wq = np.ascontiguousarray(np.asarray(L0_attn_Wq, np.float32))
    bq = np.ascontiguousarray(np.asarray(L0_attn_bq, np.float32))
    Wr = np.ascontiguousarray(np.asarray(L0_ffn_Wr, np.float32))
    W1 = np.ascontiguousarray(np.asarray(L0_ffn_W1, np.float32))
    W2 = np.ascontiguousarray(np.asarray(L0_ffn_W2, np.float32))
    xf = x.reshape(2 * SEQ, D)
    cst = _consts()
    bq_pc = np.ascontiguousarray(bq.reshape(DC, 128).T)

    if "la" not in _NC_CACHE:
        _NC_CACHE["la"] = build_la()
    la_keys = ("c_identf", "c_identb", "c_ltrib", "c_onescb", "c_onesrf",
               "c_iota8", "c_eights", "c_big")
    in_a = []
    for c in range(NCORES):
        b = c // 4
        m = {k: cst[k] for k in la_keys}
        m.update(x_batch=np.ascontiguousarray(xf[b * SEQ:(b + 1) * SEQ]),
                 x_own=np.ascontiguousarray(xf[c * TPC:(c + 1) * TPC]),
                 Wq=Wq, bq_pc=bq_pc, Wr=Wr)
        in_a.append(m)
    resA = run_bass_kernel_spmd(_NC_CACHE["la"], in_a, core_ids=list(range(NCORES))).results

    if "lb" not in _NC_CACHE:
        _NC_CACHE["lb"] = build_lb()
    counts_all = np.ascontiguousarray(
        np.concatenate([np.asarray(resA[c]["counts"], np.float32) for c in range(NCORES)], 0))
    lb_keys = ("c_identf", "c_identb", "c_ltri8s", "c_ones8", "c_onesrf",
               "c_iota8", "c_iota128", "c_big")
    in_b = []
    for e in range(NCORES):
        sel = np.zeros((E, E), np.float32)
        sel[:, e] = 1.0
        m = {k: cst[k] for k in lb_keys}
        m.update(recv=np.ascontiguousarray(np.concatenate(
                     [np.asarray(resA[c]["sendblk"])[e * PAD:(e + 1) * PAD] for c in range(NCORES)], 0)),
                 counts=counts_all, sel_bc=sel,
                 W1=np.ascontiguousarray(W1[e]), W2=np.ascontiguousarray(W2[e]))
        in_b.append(m)
    resB = run_bass_kernel_spmd(_NC_CACHE["lb"], in_b, core_ids=list(range(NCORES))).results

    if "lc" not in _NC_CACHE:
        _NC_CACHE["lc"] = build_lc()
    lc_keys = ("c_ones8", "c_onesrf", "c_iota8", "c_big")
    in_c = []
    for c in range(NCORES):
        mask = (np.arange(NCORES) < c).astype(np.float32).reshape(NCORES, 1)
        m = {k: cst[k] for k in lc_keys}
        m.update(x1=np.ascontiguousarray(np.asarray(resA[c]["x1"], np.float32)),
                 retg=np.ascontiguousarray(np.concatenate(
                     [np.asarray(resB[e]["retblk"])[c * PAD:(c + 1) * PAD] for e in range(NCORES)], 0)),
                 meta=np.ascontiguousarray(np.asarray(resA[c]["meta"], np.float32)),
                 counts=counts_all, mask_lt=mask)
        in_c.append(m)
    resC = run_bass_kernel_spmd(_NC_CACHE["lc"], in_c, core_ids=list(range(NCORES))).results

    out = np.concatenate([np.asarray(resC[c]["out"], np.float32) for c in range(NCORES)], 0)
    return out.reshape(2, SEQ, D)
